# version 01
# baseline (speedup 1.0000x reference)
import numpy as np
import concourse.bass as bass
import concourse.mybir as mybir
from concourse.bass_utils import run_bass_kernel_spmd

F32 = mybir.dt.float32
AF = mybir.ActivationFunctionType
ALU = mybir.AluOpType

D = 2048
KC = 16
T = 2048
H = 16
EPS_LN = 1e-5
EPS_AT = 1e-6
INV_D = 1.0 / D

DO_PHASES = ("ffn1", "attn", "ffn2")


def build_nc():
    nc = bass.Bass("TRN2", target_bir_lowering=False)

    x_in = nc.dram_tensor("x_in", [128, KC * T], F32, kind="ExternalInput").ap()
    x_out = nc.dram_tensor("x_out", [128, KC * T], F32, kind="ExternalOutput").ap()
    wff = nc.dram_tensor("wff", [16, 128, 2048], F32, kind="ExternalInput").ap()
    wq = nc.dram_tensor("wq", [16, 128, 2048], F32, kind="ExternalInput").ap()
    wk = nc.dram_tensor("wk", [16, 128, 2048], F32, kind="ExternalInput").ap()
    wv = nc.dram_tensor("wv", [16, 128, 2048], F32, kind="ExternalInput").ap()
    wo = nc.dram_tensor("wo", [16, 128, 2048], F32, kind="ExternalInput").ap()
    bff_d = nc.dram_tensor("bff", [128, 16], F32, kind="ExternalInput").ap()
    scf_d = nc.dram_tensor("scf", [128, 16], F32, kind="ExternalInput").ap()
    bq_d = nc.dram_tensor("bq", [128, 16], F32, kind="ExternalInput").ap()
    bk_d = nc.dram_tensor("bk", [128, 16], F32, kind="ExternalInput").ap()
    bv_d = nc.dram_tensor("bv", [128, 16], F32, kind="ExternalInput").ap()
    bo_d = nc.dram_tensor("bo", [128, 16], F32, kind="ExternalInput").ap()
    eye_d = nc.dram_tensor("eye", [128, 128], F32, kind="ExternalInput").ap()
    kvin = nc.dram_tensor("kvin", [128, H * 129], F32)
    kvout = nc.dram_tensor("kvout", [128, H * 129], F32)

    import contextlib
    stk = contextlib.ExitStack()

    def sb(name, shape):
        return stk.enter_context(nc.sbuf_tensor(name, shape, F32))

    xs = sb("xs", [128, KC * T])
    slab = sb("slab", [128, 2 * 2048])
    kv_sb = sb("kv_sb", [128, H * 129])
    nmr_b = sb("nmr_b", [128, T])
    rs_b = sb("rs_b", [128, T])
    ep = sb("ep", [128, 4 * 512])
    big = sb("big", [128, 4608])
    svec = sb("svec", [1, 4 * 512])
    eye = sb("eye_s", [128, 128])
    ones_col = sb("ones_col", [128, 1])
    ones_row = sb("ones_row", [1, 128])
    negones_row = sb("negones_row", [1, 128])
    bff = sb("bff_s", [128, 16])
    scf = sb("scf_s", [128, 16])
    bq = sb("bq_s", [128, 16])
    bk = sb("bk_s", [128, 16])
    bv = sb("bv_s", [128, 16])
    bo = sb("bo_s", [128, 16])
    b_h = sb("b_h", [128, 16])
    sc_h = sb("sc_h", [128, 16])
    wsum_all = sb("wsum_all", [128, 16])
    wsq_all = sb("wsq_all", [128, 16])
    wsa = sb("wsa", [128, 2])
    zr = sb("zr", [1, 512])
    zrecb_sb = sb("zrecb", [128, 256])

    pb = [stk.enter_context(nc.psum_tensor(f"pb{i}", [128, 512], F32)) for i in range(8)]

    SEM = {}
    for s in ("w0", "w1", "wx", "st", "mm", "dve", "act", "g", "cc"):
        SEM[s] = stk.enter_context(nc.semaphore(s))

    c = dict(mm=0, dve=0, act=0, w0=0, w1=0, wx=0, g=0, cc=0, st=0)
    SY, PE, VE, SC, GP = [], [], [], [], []
    bank_free = {i: None for i in range(8)}
    slab_par = [0]
    slab_last_use = [0, 0]

    def wait(lst, sem, val):
        if val > 0:
            lst.append(lambda e, s=SEM[sem], v=val: e.wait_ge(s, v))

    def mm(out, lhsT, rhs, start, stop, inc=False):
        v = 0
        if inc:
            c["mm"] += 1
            v = c["mm"]
        def f(e, o=out, l=lhsT, r=rhs, s=start, p=stop, iv=v):
            i = e.matmul(o, l, r, start=s, stop=p)
            if iv:
                i.then_inc(SEM["mm"], 1)
        PE.append(f)

    def transpose(out, in_, inc=False):
        v = 0
        if inc:
            c["mm"] += 1
            v = c["mm"]
        def f(e, o=out, i_=in_, iv=v):
            i = e.transpose(o, i_, eye[:, :])
            if iv:
                i.then_inc(SEM["mm"], 1)
        PE.append(f)

    def ve_op(kind, *args, inc=False):
        v = 0
        if inc:
            c["dve"] += 1
            v = c["dve"]
        def f(e, k=kind, a=args, iv=v):
            i = getattr(e, k)(*a)
            if iv:
                i.then_inc(SEM["dve"], 1)
        VE.append(f)

    def sc_op(kind, *args, inc=False, **kw):
        v = 0
        if inc:
            c["act"] += 1
            v = c["act"]
        def f(e, k=kind, a=args, kw=kw, iv=v):
            i = getattr(e, k)(*a, **kw)
            if iv:
                i.then_inc(SEM["act"], 1)
        SC.append(f)

    def sy_dma(dst, src, sem):
        c[sem] += 16
        def f(e, d=dst, s=src, sm=SEM[sem]):
            e.dma_start(out=d, in_=s).then_inc(sm, 16)
        SY.append(f)

    def slab_load(src):
        p = slab_par[0] % 2
        slab_par[0] += 1
        wait(SY, "mm", slab_last_use[p])
        sem = "w0" if p == 0 else "w1"
        sy_dma(slab[:, p * 2048:(p + 1) * 2048], src, sem)
        return p, (sem, c[sem])

    def pe_use_bank(i):
        if bank_free[i] is not None:
            wait(PE, *bank_free[i])

    ep_a = ep[:, 0:512]
    ep_y = ep[:, 512:1024]
    ep_e0 = ep[:, 1024:1536]
    ep_e1 = ep[:, 1536:2048]

    sy_dma(xs[:, :], x_in[:, :], "wx")
    sy_dma(eye[:, :], eye_d[:, :], "wx")
    for d_, s_ in ((bff, bff_d), (scf, scf_d), (bq, bq_d), (bk, bk_d), (bv, bv_d), (bo, bo_d)):
        sy_dma(d_[:, :], s_[:, :], "wx")
    ve_op("memset", ones_col[:, :], 1.0)
    ve_op("memset", ones_row[:, :], 1.0)
    ve_op("memset", negones_row[:, :], -1.0)
    wait(VE, "wx", c["wx"])
    ve_op("tensor_scalar", b_h[:, :], bff[:, :], 0.5, None, ALU.mult)
    ve_op("tensor_scalar", sc_h[:, :], scf[:, :], 0.5, None, ALU.mult, inc=True)

    def stats_phase():
        wait(PE, "dve", c["dve"])
        wait(SC, "dve", c["dve"])
        sq_parity_last_mm = [0, 0]
        for tc in range(4):
            tsl = slice(tc * 512, tc * 512 + 512)
            pe_use_bank(5)
            pe_use_bank(6)
            for k in range(16):
                xsl = xs[:, k * T + tc * 512: k * T + tc * 512 + 512]
                sq_dst = ep[:, (k % 2) * 512:(k % 2) * 512 + 512]
                wait(SC, "mm", sq_parity_last_mm[k % 2])
                if tc == 0 and k < 2:
                    wait(SC, "wx", c["wx"])
                sc_op("activation", sq_dst, xsl, AF.Square, inc=True)
                if tc == 0 and k == 0:
                    wait(PE, "wx", c["wx"])
                mm(pb[5][0:1, :], ones_col[:, :], xsl, start=(k == 0), stop=(k == 15))
                wait(PE, "act", c["act"])
                mm(pb[6][0:1, :], ones_col[:, :], sq_dst, start=(k == 0), stop=(k == 15),
                   inc=True)
                sq_parity_last_mm[k % 2] = c["mm"]
            r0, r1, r2, r3 = (svec[0:1, i * 512:(i + 1) * 512] for i in range(4))
            wait(VE, "mm", c["mm"])
            ve_op("tensor_scalar", r0, pb[5][0:1, :], INV_D, None, ALU.mult)
            ve_op("tensor_scalar", r1, pb[6][0:1, :], INV_D, None, ALU.mult, inc=True)
            pe_sum_free = ("dve", c["dve"])
            ve_op("tensor_tensor", r2, r0, r0, ALU.mult)
            ve_op("tensor_tensor", r3, r1, r2, ALU.subtract)
            ve_op("tensor_scalar", r2, r3, EPS_LN, None, ALU.add, inc=True)
            wait(SC, "dve", c["dve"])
            sc_op("activation", r3, r2, AF.Sqrt, inc=True)
            wait(VE, "act", c["act"])
            ve_op("reciprocal", r2, r3)
            ve_op("tensor_tensor", r3, r0, r2, ALU.mult, inc=True)
            wait(PE, "dve", c["dve"])
            wait(PE, *pe_sum_free)
            mm(pb[5][:, :], negones_row[:, :], r3, start=True, stop=True, inc=True)
            mm(pb[6][:, :], ones_row[:, :], r2, start=True, stop=True, inc=True)
            wait(VE, "mm", c["mm"])
            ve_op("tensor_copy", nmr_b[:, tsl], pb[5][:, :])
            ve_op("tensor_copy", rs_b[:, tsl], pb[6][:, :], inc=True)
            bank_free[5] = ("dve", c["dve"])
            bank_free[6] = ("dve", c["dve"])

    g_ctr = [0]

    def ffn_phase(compute_wsum):
        wait(VE, "act", c["act"])
        last_flush = [0]
        act3_last = [0, 0]
        for tc in range(8):
            csl = slice(tc * 256, tc * 256 + 256)
            flush_wait_act = 0
            for ot in range(16):
                p, w = slab_load(wff[ot])
                wait(PE, *w)
                if compute_wsum:
                    pe_use_bank(4)
                    for k in range(16):
                        mm(pb[4][:, 0:1], slab[:, p * 2048 + k * 128:p * 2048 + (k + 1) * 128],
                           ones_col[:, :], start=(k == 0), stop=(k == 15), inc=(k == 15))
                    wait(VE, "mm", c["mm"])
                    ve_op("tensor_copy", wsum_all[:, ot:ot + 1], pb[4][:, 0:1], inc=True)
                    bank_free[4] = ("dve", c["dve"])
                bank = g_ctr[0] % 4
                g_ctr[0] += 1
                pe_use_bank(bank)
                for k in range(16):
                    mm(pb[bank][:, 0:256],
                       slab[:, p * 2048 + k * 128:p * 2048 + (k + 1) * 128],
                       xs[:, k * T + tc * 256: k * T + tc * 256 + 256],
                       start=(k == 0), stop=(k == 15), inc=(k == 15))
                slab_last_use[p] = c["mm"]
                a_r = ep_a[:, (ot % 2) * 256:(ot % 2) * 256 + 256]
                y_r = ep_y[:, (ot % 2) * 256:(ot % 2) * 256 + 256]
                wait(VE, "mm", c["mm"])
                if act3_last[ot % 2]:
                    wait(VE, "act", act3_last[ot % 2])
                ve_op("tensor_tensor", a_r, pb[bank][:, 0:256], rs_b[:, csl], ALU.mult,
                      inc=True)
                bank_free[bank] = ("dve", c["dve"])
                ve_op("scalar_tensor_tensor", y_r, nmr_b[:, csl], wsum_all[:, ot:ot + 1],
                      a_r, ALU.mult, ALU.add, inc=True)
                wait(SC, "dve", c["dve"])
                if last_flush[0]:
                    wait(SC, "dve", last_flush[0])
                sc_op("activation", big[:, ot * 256:(ot + 1) * 256], y_r, AF.Identity,
                      bias=b_h[:, ot:ot + 1], scale=sc_h[:, ot:ot + 1], inc=True)
                act3_last[ot % 2] = c["act"]
                flush_wait_act = c["act"]
            wait(VE, "act", flush_wait_act)
            for ot in range(16):
                xsl = xs[:, ot * T + tc * 256: ot * T + tc * 256 + 256]
                ve_op("tensor_tensor", xsl, xsl, big[:, ot * 256:(ot + 1) * 256], ALU.add,
                      inc=(ot == 15))
            last_flush[0] = c["dve"]

    def fold_chunk(bank, tsl512, wsum_ap, dest, bias_ap, elu):
        wait(VE, "mm", c["mm"])
        ve_op("tensor_tensor", ep_a, pb[bank][:, 0:512], rs_b[:, tsl512], ALU.mult,
              inc=True)
        bank_free[bank] = ("dve", c["dve"])
        ve_op("scalar_tensor_tensor", ep_y, nmr_b[:, tsl512], wsum_ap, ep_a,
              ALU.mult, ALU.add)
        if elu:
            ve_op("tensor_scalar", ep_e0, ep_y, bias_ap, 0.0, ALU.add, ALU.min,
                  inc=True)
            wait(SC, "dve", c["dve"])
            sc_op("activation", ep_e1, ep_e0, AF.Exp, inc=True)
            ve_op("tensor_scalar", ep_a, ep_y, bias_ap, 0.0, ALU.add, ALU.max)
            wait(VE, "act", c["act"])
            ve_op("tensor_tensor", dest, ep_e1, ep_a, ALU.add, inc=True)
        else:
            ve_op("tensor_scalar", dest, ep_y, bias_ap, None, ALU.add, inc=True)

    def attn_phase():
        wait(VE, "act", c["act"])
        kfm = big[:, 0:512]
        kp_tm = big[:, 512:512 + 2048]
        v_tm = big[:, 2560:2560 + 2048]

        for h in range(16):
            for proj, wdram, bias_ap, wcol, tm in (
                ("k", wk, bk[:, h:h + 1], 0, kp_tm),
                ("v", wv, bv[:, h:h + 1], 1, v_tm),
            ):
                p, w = slab_load(wdram[h])
                wait(PE, *w)
                pe_use_bank(4)
                for k in range(16):
                    mm(pb[4][:, 0:1], slab[:, p * 2048 + k * 128:p * 2048 + (k + 1) * 128],
                       ones_col[:, :], start=(k == 0), stop=(k == 15), inc=(k == 15))
                wait(VE, "mm", c["mm"])
                ve_op("tensor_copy", wsa[:, wcol:wcol + 1], pb[4][:, 0:1], inc=True)
                bank_free[4] = ("dve", c["dve"])
                for tca in range(4):
                    tsl = slice(tca * 512, tca * 512 + 512)
                    bank = g_ctr[0] % 4
                    g_ctr[0] += 1
                    pe_use_bank(bank)
                    for k in range(16):
                        mm(pb[bank][:, 0:512],
                           slab[:, p * 2048 + k * 128:p * 2048 + (k + 1) * 128],
                           xs[:, k * T + tca * 512: k * T + tca * 512 + 512],
                           start=(k == 0), stop=(k == 15), inc=(k == 15))
                    if tca == 3:
                        slab_last_use[p] = c["mm"]
                    fold_chunk(bank, tsl, wsa[:, wcol:wcol + 1],
                               kfm[:, 0:512], bias_ap, elu=(proj == "k"))
                    wait(PE, "dve", c["dve"])
                    for tpair in range(2):
                        for j in range(2):
                            tl = tpair * 2 + j
                            bnk = 5 + j
                            if bank_free[bnk] is not None:
                                wait(PE, *bank_free[bnk])
                            transpose(pb[bnk][:, 0:128], kfm[:, tl * 128:(tl + 1) * 128],
                                      inc=(j == 1))
                        wait(SC, "mm", c["mm"])
                        for j in range(2):
                            tau = tca * 4 + tpair * 2 + j
                            sc_op("activation", tm[:, tau * 128:(tau + 1) * 128],
                                  pb[5 + j][:, 0:128], AF.Identity, inc=(j == 1))
                        bank_free[5] = ("act", c["act"])
                        bank_free[6] = ("act", c["act"])
            if bank_free[7] is not None:
                wait(PE, *bank_free[7])
            wait(PE, "act", c["act"])
            for tau in range(16):
                mm(pb[7][:, 0:128], kp_tm[:, tau * 128:(tau + 1) * 128],
                   v_tm[:, tau * 128:(tau + 1) * 128],
                   start=(tau == 0), stop=False)
                mm(pb[7][:, 128:129], kp_tm[:, tau * 128:(tau + 1) * 128],
                   ones_col[:, :], start=False, stop=(tau == 15), inc=(tau == 15))
            wait(VE, "mm", c["mm"])
            ve_op("tensor_copy", kv_sb[:, h * 129:(h + 1) * 129], pb[7][:, 0:129],
                  inc=True)
            bank_free[7] = ("dve", c["dve"])

        wait(GP, "dve", c["dve"])
        c["g"] += 16
        GP.append(lambda e: e.dma_start(out=kvin[:, :], in_=kv_sb[:, :]).then_inc(SEM["g"], 16))
        wait(GP, "g", c["g"])
        c["cc"] += 1
        def cc_fn(e):
            e.collective_compute(
                "AllReduce", ALU.add,
                replica_groups=[[0, 1], [2, 3], [4, 5], [6, 7]],
                ins=[kvin.ap()], outs=[kvout.ap()],
            ).then_inc(SEM["cc"], 1)
        GP.append(cc_fn)
        wait(GP, "cc", c["cc"])
        c["g"] += 16
        GP.append(lambda e: e.dma_start(out=kv_sb[:, :], in_=kvout[:, :]).then_inc(SEM["g"], 16))
        kv_ready = c["g"]

        for tc in range(8):
            for dt in range(16):
                p, w = slab_load(wq[dt])
                wait(PE, *w)
                if tc == 0:
                    pe_use_bank(4)
                    for k in range(16):
                        mm(pb[4][:, 0:1], slab[:, p * 2048 + k * 128:p * 2048 + (k + 1) * 128],
                           ones_col[:, :], start=(k == 0), stop=(k == 15), inc=(k == 15))
                    wait(VE, "mm", c["mm"])
                    ve_op("tensor_copy", wsq_all[:, dt:dt + 1], pb[4][:, 0:1], inc=True)
                    bank_free[4] = ("dve", c["dve"])
                bank = g_ctr[0] % 4
                g_ctr[0] += 1
                pe_use_bank(bank)
                for k in range(16):
                    mm(pb[bank][:, 0:256],
                       slab[:, p * 2048 + k * 128:p * 2048 + (k + 1) * 128],
                       xs[:, k * T + tc * 256: k * T + tc * 256 + 256],
                       start=(k == 0), stop=(k == 15), inc=(k == 15))
                slab_last_use[p] = c["mm"]
                csl = slice(tc * 256, tc * 256 + 256)
                a_r, y_r = ep_a[:, 0:256], ep_y[:, 0:256]
                e0_r, e1_r = ep_e0[:, 0:256], ep_e1[:, 0:256]
                dest = big[:, dt * 256:(dt + 1) * 256]
                wait(VE, "mm", c["mm"])
                ve_op("tensor_tensor", a_r, pb[bank][:, 0:256], rs_b[:, csl], ALU.mult,
                      inc=True)
                bank_free[bank] = ("dve", c["dve"])
                ve_op("scalar_tensor_tensor", y_r, nmr_b[:, csl], wsq_all[:, dt:dt + 1],
                      a_r, ALU.mult, ALU.add)
                ve_op("tensor_scalar", e0_r, y_r, bq[:, dt:dt + 1], 0.0, ALU.add, ALU.min,
                      inc=True)
                wait(SC, "dve", c["dve"])
                sc_op("activation", e1_r, e0_r, AF.Exp, inc=True)
                ve_op("tensor_scalar", a_r, y_r, bq[:, dt:dt + 1], 0.0, ALU.add, ALU.max)
                wait(VE, "act", c["act"])
                ve_op("tensor_tensor", dest, e1_r, a_r, ALU.add, inc=True)
            qp_done = c["dve"]
            if tc == 0:
                wait(PE, "g", kv_ready)
            wait(PE, "dve", qp_done)
            for h in range(16):
                if bank_free[4] is not None:
                    wait(PE, *bank_free[4])
                if bank_free[5] is not None:
                    wait(PE, *bank_free[5])
                mm(pb[4][:, 0:256], kv_sb[:, h * 129:h * 129 + 128],
                   big[:, h * 256:(h + 1) * 256], start=True, stop=True)
                mm(pb[5][0:1, 0:256], kv_sb[:, h * 129 + 128:h * 129 + 129],
                   big[:, h * 256:(h + 1) * 256], start=True, stop=True, inc=True)
                wait(VE, "mm", c["mm"])
                ve_op("tensor_scalar", zr[0:1, 0:256], pb[5][0:1, 0:256], EPS_AT, None,
                      ALU.add)
                bank_free[5] = None
                ve_op("reciprocal", zr[0:1, 256:512], zr[0:1, 0:256], inc=True)
                zrec_done = c["dve"]
                bank_free[5] = ("dve", zrec_done)
                wait(PE, "dve", zrec_done)
                if bank_free[6] is not None:
                    wait(PE, *bank_free[6])
                mm(pb[6][:, 0:256], ones_row[:, :], zr[0:1, 256:512], start=True,
                   stop=True, inc=True)
                wait(VE, "mm", c["mm"])
                ve_op("tensor_copy", zrecb_sb[:, :], pb[6][:, 0:256])
                bank_free[6] = ("dve", c["dve"] + 1)
                ve_op("tensor_tensor", big[:, h * 256:(h + 1) * 256], pb[4][:, 0:256],
                      zrecb_sb[:, :], ALU.mult, inc=True)
                bank_free[4] = ("dve", c["dve"])
            attn_done = c["dve"]
            wait(PE, "dve", attn_done)
            for ct in range(16):
                p, w = slab_load(wo[ct])
                wait(PE, *w)
                bank = g_ctr[0] % 4
                g_ctr[0] += 1
                pe_use_bank(bank)
                for k in range(16):
                    mm(pb[bank][:, 0:256],
                       slab[:, p * 2048 + k * 128:p * 2048 + (k + 1) * 128],
                       big[:, k * 256:(k + 1) * 256],
                       start=(k == 0), stop=(k == 15), inc=(k == 15))
                slab_last_use[p] = c["mm"]
                xsl = xs[:, ct * T + tc * 256: ct * T + tc * 256 + 256]
                wait(VE, "mm", c["mm"])
                ve_op("scalar_tensor_tensor", xsl, pb[bank][:, 0:256], bo[:, ct:ct + 1],
                      xsl, ALU.add, ALU.add, inc=True)
                bank_free[bank] = ("dve", c["dve"])

    if "ffn1" in DO_PHASES:
        stats_phase()
        ffn_phase(compute_wsum=True)
    if "attn" in DO_PHASES:
        stats_phase()
        attn_phase()
    if "ffn2" in DO_PHASES:
        stats_phase()
        ffn_phase(compute_wsum=("ffn1" not in DO_PHASES))

    wait(SY, "dve", c["dve"])
    c["st"] += 16
    SY.append(lambda e: e.dma_start(out=x_out[:, :], in_=xs[:, :]).then_inc(SEM["st"], 16))
    wait(SY, "st", c["st"])

    with stk, nc.Block() as block:
        @block.sync
        def _(sync):
            for f in SY:
                f(sync)

        @block.tensor
        def _(tensor):
            for f in PE:
                f(tensor)

        @block.vector
        def _(vector):
            for f in VE:
                f(vector)

        @block.scalar
        def _(scalar):
            for f in SC:
                f(scalar)

        @block.gpsimd
        def _(gpsimd):
            for f in GP:
                f(gpsimd)

    return nc


def _tile_w(w):
    wT = np.ascontiguousarray(np.asarray(w, dtype=np.float32).T)
    return np.ascontiguousarray(
        wT.reshape(16, 128, 16, 128).transpose(2, 1, 0, 3).reshape(16, 128, 2048))


def _tile_b(b):
    return np.ascontiguousarray(np.asarray(b, dtype=np.float32).reshape(16, 128).T)


def kernel(L0_attn_hidden_states, L0_attn_q_w, L0_attn_q_b, L0_attn_k_w,
           L0_attn_k_b, L0_attn_v_w, L0_attn_v_b, L0_attn_o_w, L0_attn_o_b,
           L0_ffn_weight, L0_ffn_scale_weight, L0_ffn_bias):
    x = np.asarray(L0_attn_hidden_states, dtype=np.float32)
    common = {
        "wff": _tile_w(L0_ffn_weight),
        "wq": _tile_w(L0_attn_q_w),
        "wk": _tile_w(L0_attn_k_w),
        "wv": _tile_w(L0_attn_v_w),
        "wo": _tile_w(L0_attn_o_w),
        "bff": _tile_b(L0_ffn_bias),
        "scf": _tile_b(L0_ffn_scale_weight),
        "bq": _tile_b(L0_attn_q_b),
        "bk": _tile_b(L0_attn_k_b),
        "bv": _tile_b(L0_attn_v_b),
        "bo": _tile_b(L0_attn_o_b),
        "eye": np.eye(128, dtype=np.float32),
    }
    in_maps = []
    for i in range(8):
        b, half = i // 2, i % 2
        xc = x[b, half * T:(half + 1) * T, :]
        xf = np.ascontiguousarray(
            xc.T.reshape(16, 128, T).transpose(1, 0, 2).reshape(128, KC * T))
        in_maps.append(dict(common, x_in=xf))
    nc = build_nc()
    res = run_bass_kernel_spmd(nc, in_maps, core_ids=list(range(8)))
    out = np.empty((4, 4096, D), dtype=np.float32)
    for i in range(8):
        b, half = i // 2, i % 2
        o = res.results[i]["x_out"]
        xT = o.reshape(128, 16, T).transpose(1, 0, 2).reshape(D, T)
        out[b, half * T:(half + 1) * T, :] = xT.T
    return out


# version 02
# speedup vs baseline: 1.0096x; 1.0096x over previous
import numpy as np
import concourse.bass as bass
import concourse.mybir as mybir
from concourse.bass_utils import run_bass_kernel_spmd

F32 = mybir.dt.float32
AF = mybir.ActivationFunctionType
ALU = mybir.AluOpType

D = 2048
KC = 16
T = 2048
H = 16
EPS_LN = 1e-5
EPS_AT = 1e-6
INV_D = 1.0 / D

DO_PHASES = ("ffn1", "attn", "ffn2")


def build_nc():
    nc = bass.Bass("TRN2", target_bir_lowering=False, detect_race_conditions=False)

    x_in = nc.dram_tensor("x_in", [128, KC * T], F32, kind="ExternalInput").ap()
    x_out = nc.dram_tensor("x_out", [128, KC * T], F32, kind="ExternalOutput").ap()
    wff = nc.dram_tensor("wff", [16, 128, 2048], F32, kind="ExternalInput").ap()
    wq = nc.dram_tensor("wq", [16, 128, 2048], F32, kind="ExternalInput").ap()
    wk = nc.dram_tensor("wk", [16, 128, 2048], F32, kind="ExternalInput").ap()
    wv = nc.dram_tensor("wv", [16, 128, 2048], F32, kind="ExternalInput").ap()
    wo = nc.dram_tensor("wo", [16, 128, 2048], F32, kind="ExternalInput").ap()
    bff_d = nc.dram_tensor("bff", [128, 16], F32, kind="ExternalInput").ap()
    scf_d = nc.dram_tensor("scf", [128, 16], F32, kind="ExternalInput").ap()
    bq_d = nc.dram_tensor("bq", [128, 16], F32, kind="ExternalInput").ap()
    bk_d = nc.dram_tensor("bk", [128, 16], F32, kind="ExternalInput").ap()
    bv_d = nc.dram_tensor("bv", [128, 16], F32, kind="ExternalInput").ap()
    bo_d = nc.dram_tensor("bo", [128, 16], F32, kind="ExternalInput").ap()
    eye_d = nc.dram_tensor("eye", [128, 128], F32, kind="ExternalInput").ap()
    kvin = nc.dram_tensor("kvin", [128, H * 129], F32)
    kvout = nc.dram_tensor("kvout", [128, H * 129], F32)

    import contextlib
    stk = contextlib.ExitStack()

    def sb(name, shape):
        return stk.enter_context(nc.sbuf_tensor(name, shape, F32))

    xs = sb("xs", [128, KC * T])
    slab = sb("slab", [128, 2 * 2048])
    kv_sb = sb("kv_sb", [128, H * 129])
    nmr_b = sb("nmr_b", [128, T])
    rs_b = sb("rs_b", [128, T])
    ep = sb("ep", [128, 4 * 512])
    big = sb("big", [128, 4608])
    svec = sb("svec", [1, 4 * 512])
    eye = sb("eye_s", [128, 128])
    ones_col = sb("ones_col", [128, 1])
    ones_row = sb("ones_row", [1, 128])
    negones_row = sb("negones_row", [1, 128])
    bff = sb("bff_s", [128, 16])
    scf = sb("scf_s", [128, 16])
    bq = sb("bq_s", [128, 16])
    bk = sb("bk_s", [128, 16])
    bv = sb("bv_s", [128, 16])
    bo = sb("bo_s", [128, 16])
    b_h = sb("b_h", [128, 16])
    sc_h = sb("sc_h", [128, 16])
    wsum_all = sb("wsum_all", [128, 16])
    wsq_all = sb("wsq_all", [128, 16])
    wsa = sb("wsa", [128, 2])
    zr = sb("zr", [1, 512])
    zrecb_sb = sb("zrecb", [128, 256])

    pb = [stk.enter_context(nc.psum_tensor(f"pb{i}", [128, 512], F32)) for i in range(8)]

    SEM = {}
    for s in ("w0", "w1", "wx", "st", "mm", "dve", "act", "g", "cc"):
        SEM[s] = stk.enter_context(nc.semaphore(s))

    c = dict(mm=0, dve=0, act=0, w0=0, w1=0, wx=0, g=0, cc=0, st=0)
    SY, PE, VE, SC, GP = [], [], [], [], []
    bank_free = {i: None for i in range(8)}
    slab_par = [0]
    slab_last_use = [0, 0]

    def wait(lst, sem, val):
        if val > 0:
            lst.append(lambda e, s=SEM[sem], v=val: e.wait_ge(s, v))

    def mm(out, lhsT, rhs, start, stop, inc=False):
        v = 0
        if inc:
            c["mm"] += 1
            v = c["mm"]
        def f(e, o=out, l=lhsT, r=rhs, s=start, p=stop, iv=v):
            i = e.matmul(o, l, r, start=s, stop=p)
            if iv:
                i.then_inc(SEM["mm"], 1)
        PE.append(f)

    def transpose(out, in_, inc=False):
        v = 0
        if inc:
            c["mm"] += 1
            v = c["mm"]
        def f(e, o=out, i_=in_, iv=v):
            i = e.transpose(o, i_, eye[:, :])
            if iv:
                i.then_inc(SEM["mm"], 1)
        PE.append(f)

    def ve_op(kind, *args, inc=False):
        v = 0
        if inc:
            c["dve"] += 1
            v = c["dve"]
        def f(e, k=kind, a=args, iv=v):
            i = getattr(e, k)(*a)
            if iv:
                i.then_inc(SEM["dve"], 1)
        VE.append(f)

    def sc_op(kind, *args, inc=False, **kw):
        v = 0
        if inc:
            c["act"] += 1
            v = c["act"]
        def f(e, k=kind, a=args, kw=kw, iv=v):
            i = getattr(e, k)(*a, **kw)
            if iv:
                i.then_inc(SEM["act"], 1)
        SC.append(f)

    def sy_dma(dst, src, sem):
        c[sem] += 16
        def f(e, d=dst, s=src, sm=SEM[sem]):
            e.dma_start(out=d, in_=s).then_inc(sm, 16)
        SY.append(f)

    def slab_load(src):
        p = slab_par[0] % 2
        slab_par[0] += 1
        wait(SY, "mm", slab_last_use[p])
        sem = "w0" if p == 0 else "w1"
        sy_dma(slab[:, p * 2048:(p + 1) * 2048], src, sem)
        return p, (sem, c[sem])

    def pe_use_bank(i):
        if bank_free[i] is not None:
            wait(PE, *bank_free[i])

    ep_a = ep[:, 0:512]
    ep_y = ep[:, 512:1024]
    ep_e0 = ep[:, 1024:1536]
    ep_e1 = ep[:, 1536:2048]

    sy_dma(xs[:, :], x_in[:, :], "wx")
    sy_dma(eye[:, :], eye_d[:, :], "wx")
    for d_, s_ in ((bff, bff_d), (scf, scf_d), (bq, bq_d), (bk, bk_d), (bv, bv_d), (bo, bo_d)):
        sy_dma(d_[:, :], s_[:, :], "wx")
    ve_op("memset", ones_col[:, :], 1.0)
    ve_op("memset", ones_row[:, :], 1.0)
    ve_op("memset", negones_row[:, :], -1.0)
    wait(VE, "wx", c["wx"])
    ve_op("tensor_scalar", b_h[:, :], bff[:, :], 0.5, None, ALU.mult)
    ve_op("tensor_scalar", sc_h[:, :], scf[:, :], 0.5, None, ALU.mult, inc=True)

    def stats_phase():
        wait(PE, "dve", c["dve"])
        wait(SC, "dve", c["dve"])
        sq_parity_last_mm = [0, 0]
        for tc in range(4):
            tsl = slice(tc * 512, tc * 512 + 512)
            pe_use_bank(5)
            pe_use_bank(6)
            for k in range(16):
                xsl = xs[:, k * T + tc * 512: k * T + tc * 512 + 512]
                sq_dst = ep[:, (k % 2) * 512:(k % 2) * 512 + 512]
                wait(SC, "mm", sq_parity_last_mm[k % 2])
                if tc == 0 and k < 2:
                    wait(SC, "wx", c["wx"])
                sc_op("activation", sq_dst, xsl, AF.Square, inc=True)
                if tc == 0 and k == 0:
                    wait(PE, "wx", c["wx"])
                mm(pb[5][0:1, :], ones_col[:, :], xsl, start=(k == 0), stop=(k == 15))
                wait(PE, "act", c["act"])
                mm(pb[6][0:1, :], ones_col[:, :], sq_dst, start=(k == 0), stop=(k == 15),
                   inc=True)
                sq_parity_last_mm[k % 2] = c["mm"]
            r0, r1, r2, r3 = (svec[0:1, i * 512:(i + 1) * 512] for i in range(4))
            wait(VE, "mm", c["mm"])
            ve_op("tensor_scalar", r0, pb[5][0:1, :], INV_D, None, ALU.mult)
            ve_op("tensor_scalar", r1, pb[6][0:1, :], INV_D, None, ALU.mult, inc=True)
            pe_sum_free = ("dve", c["dve"])
            ve_op("tensor_tensor", r2, r0, r0, ALU.mult)
            ve_op("tensor_tensor", r3, r1, r2, ALU.subtract)
            ve_op("tensor_scalar", r2, r3, EPS_LN, None, ALU.add, inc=True)
            wait(SC, "dve", c["dve"])
            sc_op("activation", r3, r2, AF.Sqrt, inc=True)
            wait(VE, "act", c["act"])
            ve_op("reciprocal", r2, r3)
            ve_op("tensor_tensor", r3, r0, r2, ALU.mult, inc=True)
            wait(PE, "dve", c["dve"])
            wait(PE, *pe_sum_free)
            mm(pb[5][:, :], negones_row[:, :], r3, start=True, stop=True, inc=True)
            mm(pb[6][:, :], ones_row[:, :], r2, start=True, stop=True, inc=True)
            wait(VE, "mm", c["mm"])
            ve_op("tensor_copy", nmr_b[:, tsl], pb[5][:, :])
            ve_op("tensor_copy", rs_b[:, tsl], pb[6][:, :], inc=True)
            bank_free[5] = ("dve", c["dve"])
            bank_free[6] = ("dve", c["dve"])

    g_ctr = [0]

    def ffn_phase(compute_wsum):
        wait(VE, "act", c["act"])
        last_flush = [0]
        act3_last = [0, 0]
        for tc in range(8):
            csl = slice(tc * 256, tc * 256 + 256)
            flush_wait_act = 0
            for ot in range(16):
                p, w = slab_load(wff[ot])
                wait(PE, *w)
                if compute_wsum:
                    pe_use_bank(4)
                    for k in range(16):
                        mm(pb[4][:, 0:1], slab[:, p * 2048 + k * 128:p * 2048 + (k + 1) * 128],
                           ones_col[:, :], start=(k == 0), stop=(k == 15), inc=(k == 15))
                    wait(VE, "mm", c["mm"])
                    ve_op("tensor_copy", wsum_all[:, ot:ot + 1], pb[4][:, 0:1], inc=True)
                    bank_free[4] = ("dve", c["dve"])
                bank = g_ctr[0] % 4
                g_ctr[0] += 1
                pe_use_bank(bank)
                for k in range(16):
                    mm(pb[bank][:, 0:256],
                       slab[:, p * 2048 + k * 128:p * 2048 + (k + 1) * 128],
                       xs[:, k * T + tc * 256: k * T + tc * 256 + 256],
                       start=(k == 0), stop=(k == 15), inc=(k == 15))
                slab_last_use[p] = c["mm"]
                a_r = ep_a[:, (ot % 2) * 256:(ot % 2) * 256 + 256]
                y_r = ep_y[:, (ot % 2) * 256:(ot % 2) * 256 + 256]
                wait(VE, "mm", c["mm"])
                if act3_last[ot % 2]:
                    wait(VE, "act", act3_last[ot % 2])
                ve_op("tensor_tensor", a_r, pb[bank][:, 0:256], rs_b[:, csl], ALU.mult,
                      inc=True)
                bank_free[bank] = ("dve", c["dve"])
                ve_op("scalar_tensor_tensor", y_r, nmr_b[:, csl], wsum_all[:, ot:ot + 1],
                      a_r, ALU.mult, ALU.add, inc=True)
                wait(SC, "dve", c["dve"])
                if last_flush[0]:
                    wait(SC, "dve", last_flush[0])
                sc_op("activation", big[:, ot * 256:(ot + 1) * 256], y_r, AF.Identity,
                      bias=b_h[:, ot:ot + 1], scale=sc_h[:, ot:ot + 1], inc=True)
                act3_last[ot % 2] = c["act"]
                flush_wait_act = c["act"]
            wait(VE, "act", flush_wait_act)
            for ot in range(16):
                xsl = xs[:, ot * T + tc * 256: ot * T + tc * 256 + 256]
                ve_op("tensor_tensor", xsl, xsl, big[:, ot * 256:(ot + 1) * 256], ALU.add,
                      inc=(ot == 15))
            last_flush[0] = c["dve"]

    def fold_chunk(bank, tsl512, wsum_ap, dest, bias_ap, elu):
        wait(VE, "mm", c["mm"])
        ve_op("tensor_tensor", ep_a, pb[bank][:, 0:512], rs_b[:, tsl512], ALU.mult,
              inc=True)
        bank_free[bank] = ("dve", c["dve"])
        ve_op("scalar_tensor_tensor", ep_y, nmr_b[:, tsl512], wsum_ap, ep_a,
              ALU.mult, ALU.add)
        if elu:
            ve_op("tensor_scalar", ep_e0, ep_y, bias_ap, 0.0, ALU.add, ALU.min,
                  inc=True)
            wait(SC, "dve", c["dve"])
            sc_op("activation", ep_e1, ep_e0, AF.Exp, inc=True)
            ve_op("tensor_scalar", ep_a, ep_y, bias_ap, 0.0, ALU.add, ALU.max)
            wait(VE, "act", c["act"])
            ve_op("tensor_tensor", dest, ep_e1, ep_a, ALU.add, inc=True)
        else:
            ve_op("tensor_scalar", dest, ep_y, bias_ap, None, ALU.add, inc=True)

    def attn_phase():
        wait(VE, "act", c["act"])
        kfm = big[:, 0:512]
        kp_tm = big[:, 512:512 + 2048]
        v_tm = big[:, 2560:2560 + 2048]

        for h in range(16):
            for proj, wdram, bias_ap, wcol, tm in (
                ("k", wk, bk[:, h:h + 1], 0, kp_tm),
                ("v", wv, bv[:, h:h + 1], 1, v_tm),
            ):
                p, w = slab_load(wdram[h])
                wait(PE, *w)
                pe_use_bank(4)
                for k in range(16):
                    mm(pb[4][:, 0:1], slab[:, p * 2048 + k * 128:p * 2048 + (k + 1) * 128],
                       ones_col[:, :], start=(k == 0), stop=(k == 15), inc=(k == 15))
                wait(VE, "mm", c["mm"])
                ve_op("tensor_copy", wsa[:, wcol:wcol + 1], pb[4][:, 0:1], inc=True)
                bank_free[4] = ("dve", c["dve"])
                for tca in range(4):
                    tsl = slice(tca * 512, tca * 512 + 512)
                    bank = g_ctr[0] % 4
                    g_ctr[0] += 1
                    pe_use_bank(bank)
                    for k in range(16):
                        mm(pb[bank][:, 0:512],
                           slab[:, p * 2048 + k * 128:p * 2048 + (k + 1) * 128],
                           xs[:, k * T + tca * 512: k * T + tca * 512 + 512],
                           start=(k == 0), stop=(k == 15), inc=(k == 15))
                    if tca == 3:
                        slab_last_use[p] = c["mm"]
                    fold_chunk(bank, tsl, wsa[:, wcol:wcol + 1],
                               kfm[:, 0:512], bias_ap, elu=(proj == "k"))
                    wait(PE, "dve", c["dve"])
                    for tpair in range(2):
                        for j in range(2):
                            tl = tpair * 2 + j
                            bnk = 5 + j
                            if bank_free[bnk] is not None:
                                wait(PE, *bank_free[bnk])
                            transpose(pb[bnk][:, 0:128], kfm[:, tl * 128:(tl + 1) * 128],
                                      inc=(j == 1))
                        wait(SC, "mm", c["mm"])
                        for j in range(2):
                            tau = tca * 4 + tpair * 2 + j
                            sc_op("activation", tm[:, tau * 128:(tau + 1) * 128],
                                  pb[5 + j][:, 0:128], AF.Identity, inc=(j == 1))
                        bank_free[5] = ("act", c["act"])
                        bank_free[6] = ("act", c["act"])
            if bank_free[7] is not None:
                wait(PE, *bank_free[7])
            wait(PE, "act", c["act"])
            for tau in range(16):
                mm(pb[7][:, 0:128], kp_tm[:, tau * 128:(tau + 1) * 128],
                   v_tm[:, tau * 128:(tau + 1) * 128],
                   start=(tau == 0), stop=False)
                mm(pb[7][:, 128:129], kp_tm[:, tau * 128:(tau + 1) * 128],
                   ones_col[:, :], start=False, stop=(tau == 15), inc=(tau == 15))
            wait(VE, "mm", c["mm"])
            ve_op("tensor_copy", kv_sb[:, h * 129:(h + 1) * 129], pb[7][:, 0:129],
                  inc=True)
            bank_free[7] = ("dve", c["dve"])

        wait(GP, "dve", c["dve"])
        c["g"] += 16
        GP.append(lambda e: e.dma_start(out=kvin[:, :], in_=kv_sb[:, :]).then_inc(SEM["g"], 16))
        wait(GP, "g", c["g"])
        c["cc"] += 1
        def cc_fn(e):
            e.collective_compute(
                "AllReduce", ALU.add,
                replica_groups=[[0, 1], [2, 3], [4, 5], [6, 7]],
                ins=[kvin.ap()], outs=[kvout.ap()],
            ).then_inc(SEM["cc"], 1)
        GP.append(cc_fn)
        wait(GP, "cc", c["cc"])
        c["g"] += 16
        GP.append(lambda e: e.dma_start(out=kv_sb[:, :], in_=kvout[:, :]).then_inc(SEM["g"], 16))
        kv_ready = c["g"]

        for tc in range(8):
            for dt in range(16):
                p, w = slab_load(wq[dt])
                wait(PE, *w)
                if tc == 0:
                    pe_use_bank(4)
                    for k in range(16):
                        mm(pb[4][:, 0:1], slab[:, p * 2048 + k * 128:p * 2048 + (k + 1) * 128],
                           ones_col[:, :], start=(k == 0), stop=(k == 15), inc=(k == 15))
                    wait(VE, "mm", c["mm"])
                    ve_op("tensor_copy", wsq_all[:, dt:dt + 1], pb[4][:, 0:1], inc=True)
                    bank_free[4] = ("dve", c["dve"])
                bank = g_ctr[0] % 4
                g_ctr[0] += 1
                pe_use_bank(bank)
                for k in range(16):
                    mm(pb[bank][:, 0:256],
                       slab[:, p * 2048 + k * 128:p * 2048 + (k + 1) * 128],
                       xs[:, k * T + tc * 256: k * T + tc * 256 + 256],
                       start=(k == 0), stop=(k == 15), inc=(k == 15))
                slab_last_use[p] = c["mm"]
                csl = slice(tc * 256, tc * 256 + 256)
                a_r, y_r = ep_a[:, 0:256], ep_y[:, 0:256]
                e0_r, e1_r = ep_e0[:, 0:256], ep_e1[:, 0:256]
                dest = big[:, dt * 256:(dt + 1) * 256]
                wait(VE, "mm", c["mm"])
                ve_op("tensor_tensor", a_r, pb[bank][:, 0:256], rs_b[:, csl], ALU.mult,
                      inc=True)
                bank_free[bank] = ("dve", c["dve"])
                ve_op("scalar_tensor_tensor", y_r, nmr_b[:, csl], wsq_all[:, dt:dt + 1],
                      a_r, ALU.mult, ALU.add)
                ve_op("tensor_scalar", e0_r, y_r, bq[:, dt:dt + 1], 0.0, ALU.add, ALU.min,
                      inc=True)
                wait(SC, "dve", c["dve"])
                sc_op("activation", e1_r, e0_r, AF.Exp, inc=True)
                ve_op("tensor_scalar", a_r, y_r, bq[:, dt:dt + 1], 0.0, ALU.add, ALU.max)
                wait(VE, "act", c["act"])
                ve_op("tensor_tensor", dest, e1_r, a_r, ALU.add, inc=True)
            qp_done = c["dve"]
            if tc == 0:
                wait(PE, "g", kv_ready)
            wait(PE, "dve", qp_done)
            for h in range(16):
                if bank_free[4] is not None:
                    wait(PE, *bank_free[4])
                if bank_free[5] is not None:
                    wait(PE, *bank_free[5])
                mm(pb[4][:, 0:256], kv_sb[:, h * 129:h * 129 + 128],
                   big[:, h * 256:(h + 1) * 256], start=True, stop=True)
                mm(pb[5][0:1, 0:256], kv_sb[:, h * 129 + 128:h * 129 + 129],
                   big[:, h * 256:(h + 1) * 256], start=True, stop=True, inc=True)
                wait(VE, "mm", c["mm"])
                ve_op("tensor_scalar", zr[0:1, 0:256], pb[5][0:1, 0:256], EPS_AT, None,
                      ALU.add)
                bank_free[5] = None
                ve_op("reciprocal", zr[0:1, 256:512], zr[0:1, 0:256], inc=True)
                zrec_done = c["dve"]
                bank_free[5] = ("dve", zrec_done)
                wait(PE, "dve", zrec_done)
                if bank_free[6] is not None:
                    wait(PE, *bank_free[6])
                mm(pb[6][:, 0:256], ones_row[:, :], zr[0:1, 256:512], start=True,
                   stop=True, inc=True)
                wait(VE, "mm", c["mm"])
                ve_op("tensor_copy", zrecb_sb[:, :], pb[6][:, 0:256])
                bank_free[6] = ("dve", c["dve"] + 1)
                ve_op("tensor_tensor", big[:, h * 256:(h + 1) * 256], pb[4][:, 0:256],
                      zrecb_sb[:, :], ALU.mult, inc=True)
                bank_free[4] = ("dve", c["dve"])
            attn_done = c["dve"]
            wait(PE, "dve", attn_done)
            for ct in range(16):
                p, w = slab_load(wo[ct])
                wait(PE, *w)
                bank = g_ctr[0] % 4
                g_ctr[0] += 1
                pe_use_bank(bank)
                for k in range(16):
                    mm(pb[bank][:, 0:256],
                       slab[:, p * 2048 + k * 128:p * 2048 + (k + 1) * 128],
                       big[:, k * 256:(k + 1) * 256],
                       start=(k == 0), stop=(k == 15), inc=(k == 15))
                slab_last_use[p] = c["mm"]
                xsl = xs[:, ct * T + tc * 256: ct * T + tc * 256 + 256]
                wait(VE, "mm", c["mm"])
                ve_op("scalar_tensor_tensor", xsl, pb[bank][:, 0:256], bo[:, ct:ct + 1],
                      xsl, ALU.add, ALU.add, inc=True)
                bank_free[bank] = ("dve", c["dve"])

    if "ffn1" in DO_PHASES:
        stats_phase()
        ffn_phase(compute_wsum=True)
    if "attn" in DO_PHASES:
        stats_phase()
        attn_phase()
    if "ffn2" in DO_PHASES:
        stats_phase()
        ffn_phase(compute_wsum=("ffn1" not in DO_PHASES))

    wait(SY, "dve", c["dve"])
    c["st"] += 16
    SY.append(lambda e: e.dma_start(out=x_out[:, :], in_=xs[:, :]).then_inc(SEM["st"], 16))
    wait(SY, "st", c["st"])

    with stk, nc.Block() as block:
        @block.sync
        def _(sync):
            for f in SY:
                f(sync)

        @block.tensor
        def _(tensor):
            for f in PE:
                f(tensor)

        @block.vector
        def _(vector):
            for f in VE:
                f(vector)

        @block.scalar
        def _(scalar):
            for f in SC:
                f(scalar)

        @block.gpsimd
        def _(gpsimd):
            for f in GP:
                f(gpsimd)

    return nc


def _tile_w(w):
    wT = np.ascontiguousarray(np.asarray(w, dtype=np.float32).T)
    return np.ascontiguousarray(
        wT.reshape(16, 128, 16, 128).transpose(2, 1, 0, 3).reshape(16, 128, 2048))


def _tile_b(b):
    return np.ascontiguousarray(np.asarray(b, dtype=np.float32).reshape(16, 128).T)


def kernel(L0_attn_hidden_states, L0_attn_q_w, L0_attn_q_b, L0_attn_k_w,
           L0_attn_k_b, L0_attn_v_w, L0_attn_v_b, L0_attn_o_w, L0_attn_o_b,
           L0_ffn_weight, L0_ffn_scale_weight, L0_ffn_bias):
    x = np.asarray(L0_attn_hidden_states, dtype=np.float32)
    common = {
        "wff": _tile_w(L0_ffn_weight),
        "wq": _tile_w(L0_attn_q_w),
        "wk": _tile_w(L0_attn_k_w),
        "wv": _tile_w(L0_attn_v_w),
        "wo": _tile_w(L0_attn_o_w),
        "bff": _tile_b(L0_ffn_bias),
        "scf": _tile_b(L0_ffn_scale_weight),
        "bq": _tile_b(L0_attn_q_b),
        "bk": _tile_b(L0_attn_k_b),
        "bv": _tile_b(L0_attn_v_b),
        "bo": _tile_b(L0_attn_o_b),
        "eye": np.eye(128, dtype=np.float32),
    }
    in_maps = []
    for i in range(8):
        b, half = i // 2, i % 2
        xc = x[b, half * T:(half + 1) * T, :]
        xf = np.ascontiguousarray(
            xc.T.reshape(16, 128, T).transpose(1, 0, 2).reshape(128, KC * T))
        in_maps.append(dict(common, x_in=xf))
    nc = build_nc()
    res = run_bass_kernel_spmd(nc, in_maps, core_ids=list(range(8)))
    out = np.empty((4, 4096, D), dtype=np.float32)
    for i in range(8):
        b, half = i // 2, i % 2
        o = res.results[i]["x_out"]
        xT = o.reshape(128, 16, T).transpose(1, 0, 2).reshape(D, T)
        out[b, half * T:(half + 1) * T, :] = xT.T
    return out


# version 03
# speedup vs baseline: 1.0819x; 1.0716x over previous
import numpy as np
import concourse.bass as bass
import concourse.mybir as mybir
from concourse.bass_utils import run_bass_kernel_spmd

F32 = mybir.dt.float32
AF = mybir.ActivationFunctionType
ALU = mybir.AluOpType

D = 2048
KC = 16
T = 2048
H = 16
EPS_LN = 1e-5
EPS_AT = 1e-6
INV_D = 1.0 / D

DO_PHASES = ("ffn1", "attn", "ffn2")


def build_nc():
    nc = bass.Bass("TRN2", target_bir_lowering=False, detect_race_conditions=False,
                   dynamic_dma_scratch_size=4096)

    x_in = nc.dram_tensor("x_in", [128, KC * T], F32, kind="ExternalInput").ap()
    x_out = nc.dram_tensor("x_out", [128, KC * T], F32, kind="ExternalOutput").ap()
    wff = nc.dram_tensor("wff", [16, 128, 2048], F32, kind="ExternalInput").ap()
    wq = nc.dram_tensor("wq", [16, 128, 2048], F32, kind="ExternalInput").ap()
    wk = nc.dram_tensor("wk", [16, 128, 2048], F32, kind="ExternalInput").ap()
    wv = nc.dram_tensor("wv", [16, 128, 2048], F32, kind="ExternalInput").ap()
    wo = nc.dram_tensor("wo", [16, 128, 2048], F32, kind="ExternalInput").ap()
    bias_d = nc.dram_tensor("bias6", [6, 128, 16], F32, kind="ExternalInput").ap()
    eye_d = nc.dram_tensor("eye", [128, 128], F32, kind="ExternalInput").ap()
    kvin = nc.dram_tensor("kvin", [128, H * 129], F32)
    kvout = nc.dram_tensor("kvout", [128, H * 129], F32)

    import contextlib
    stk = contextlib.ExitStack()

    def sb(name, shape):
        return stk.enter_context(nc.sbuf_tensor(name, shape, F32))

    xs = sb("xs", [128, KC * T])
    slab = sb("slab", [128, 2 * 2048])
    kv_sb = sb("kv_sb", [128, H * 129])
    nmr_b = sb("nmr_b", [128, T])
    rs_b = sb("rs_b", [128, T])
    ep = sb("ep", [128, 4 * 512])
    big = sb("big", [128, 8192])
    arena = sb("arena", [128, 768])

    eye = arena[:, 0:128]
    ones_col = arena[:, 128:129]
    ones_row = arena[0:1, 144:272]
    negones_row = arena[0:1, 272:400]
    bff = arena[:, 400:416]
    scf = arena[:, 416:432]
    bq = arena[:, 432:448]
    bk = arena[:, 448:464]
    bv = arena[:, 464:480]
    bo = arena[:, 480:496]
    b_h = arena[:, 496:512]
    sc_h = arena[:, 512:528]
    wsum_all = arena[:, 528:544]
    wsq_all = arena[:, 544:560]
    wsa = arena[:, 560:562]
    wsv = arena[0:1, 576:704]

    ep_a = ep[:, 0:512]
    ep_y = ep[:, 512:1024]
    ep_e0 = ep[:, 1024:1536]
    ep_e1 = ep[:, 1536:2048]
    r0 = ep[0:1, 0:512]
    r1 = ep[0:1, 512:1024]
    r2 = ep[0:1, 1024:1536]
    r3 = ep[0:1, 1536:2048]

    pb = [stk.enter_context(nc.psum_tensor(f"pb{i}", [128, 512], F32)) for i in range(8)]

    SEM = {}
    for s in ("w0", "w1", "wx", "st", "mm", "dve", "act", "g", "cc"):
        SEM[s] = stk.enter_context(nc.semaphore(s))

    c = dict(mm=0, dve=0, act=0, w0=0, w1=0, wx=0, g=0, cc=0, st=0)
    SY, PE, VE, SC, GP = [], [], [], [], []
    bank_free = {i: None for i in range(8)}
    slab_par = [0]
    slab_last_use = [0, 0]

    def wait(lst, sem, val):
        if val > 0:
            lst.append(lambda e, s=SEM[sem], v=val: e.wait_ge(s, v))

    def mm(out, lhsT, rhs, start, stop, inc=False):
        v = 0
        if inc:
            c["mm"] += 1
            v = c["mm"]
        def f(e, o=out, l=lhsT, r=rhs, s=start, p=stop, iv=v):
            i = e.matmul(o, l, r, start=s, stop=p)
            if iv:
                i.then_inc(SEM["mm"], 1)
        PE.append(f)

    def transpose(out, in_, ident, inc=False):
        v = 0
        if inc:
            c["mm"] += 1
            v = c["mm"]
        def f(e, o=out, i_=in_, idn=ident, iv=v):
            i = e.transpose(o, i_, idn)
            if iv:
                i.then_inc(SEM["mm"], 1)
        PE.append(f)

    def ve_op(kind, *args, inc=False):
        v = 0
        if inc:
            c["dve"] += 1
            v = c["dve"]
        def f(e, k=kind, a=args, iv=v):
            i = getattr(e, k)(*a)
            if iv:
                i.then_inc(SEM["dve"], 1)
        VE.append(f)

    def sc_op(kind, *args, inc=False, **kw):
        v = 0
        if inc:
            c["act"] += 1
            v = c["act"]
        def f(e, k=kind, a=args, kw=kw, iv=v):
            i = getattr(e, k)(*a, **kw)
            if iv:
                i.then_inc(SEM["act"], 1)
        SC.append(f)

    def sy_dma(dst, src, sem):
        c[sem] += 16
        def f(e, d=dst, s=src, sm=SEM[sem]):
            e.dma_start(out=d, in_=s).then_inc(sm, 16)
        SY.append(f)

    def slab_load(src):
        p = slab_par[0] % 2
        slab_par[0] += 1
        wait(SY, "mm", slab_last_use[p])
        sem = "w0" if p == 0 else "w1"
        sy_dma(slab[:, p * 2048:(p + 1) * 2048], src, sem)
        return p, (sem, c[sem])

    def pe_use_bank(i):
        if bank_free[i] is not None:
            wait(PE, *bank_free[i])

    g_ctr = [0]

    def proj_group(slab_p, k_rhs_fn, n=512, bank_override=None):
        if bank_override is None:
            bank = g_ctr[0] % 4
            g_ctr[0] += 1
        else:
            bank = bank_override
        pe_use_bank(bank)
        for k in range(16):
            mm(pb[bank][:, 0:n],
               slab[:, slab_p * 2048 + k * 128:slab_p * 2048 + (k + 1) * 128],
               k_rhs_fn(k),
               start=(k == 0), stop=(k == 15), inc=(k == 15))
        return bank

    def wsum_part1(slab_p):
        pe_use_bank(4)
        for k in range(16):
            mm(pb[4][0:1, 0:128], ones_col,
               slab[:, slab_p * 2048 + k * 128:slab_p * 2048 + (k + 1) * 128],
               start=(k == 0), stop=(k == 15), inc=(k == 15))
        wait(VE, "mm", c["mm"])
        ve_op("tensor_copy", wsv, pb[4][0:1, 0:128], inc=True)

    def wsum_part2(dest_col):
        wait(PE, "dve", c["dve"])
        transpose(pb[4][:, 0:1], wsv, eye[0:1, 0:1], inc=True)
        wait(VE, "mm", c["mm"])
        ve_op("tensor_copy", dest_col, pb[4][:, 0:1], inc=True)
        bank_free[4] = ("dve", c["dve"])

    sy_dma(xs[:, :], x_in[:, :], "wx")
    sy_dma(eye, eye_d[:, :], "wx")
    for j, d_ in enumerate((bff, scf, bq, bk, bv, bo)):
        sy_dma(d_, bias_d[j], "wx")
    ve_op("memset", ones_col, 1.0)
    ve_op("memset", ones_row, 1.0)
    ve_op("memset", negones_row, -1.0)
    wait(VE, "wx", c["wx"])
    ve_op("tensor_scalar", b_h, bff, 0.5, None, ALU.mult)
    ve_op("tensor_scalar", sc_h, scf, 0.5, None, ALU.mult, inc=True)

    def stats_phase():
        wait(PE, "dve", c["dve"])
        wait(SC, "dve", c["dve"])
        wait(VE, "act", c["act"])
        sq_parity_last_mm = [0, 0]
        prev_math_dve = [0]
        for tc in range(4):
            tsl = slice(tc * 512, tc * 512 + 512)
            pe_use_bank(5)
            pe_use_bank(6)
            for k in range(16):
                xsl = xs[:, k * T + tc * 512: k * T + tc * 512 + 512]
                sq_dst = ep[:, (k % 2) * 512:(k % 2) * 512 + 512]
                wait(SC, "mm", sq_parity_last_mm[k % 2])
                if k == 0:
                    wait(SC, "dve", prev_math_dve[0])
                if tc == 0 and k < 2:
                    wait(SC, "wx", c["wx"])
                sc_op("activation", sq_dst, xsl, AF.Square, inc=True)
                if tc == 0 and k == 0:
                    wait(PE, "wx", c["wx"])
                mm(pb[5][0:1, :], ones_col, xsl, start=(k == 0), stop=(k == 15))
                wait(PE, "act", c["act"])
                mm(pb[6][0:1, :], ones_col, sq_dst, start=(k == 0), stop=(k == 15),
                   inc=True)
                sq_parity_last_mm[k % 2] = c["mm"]
            wait(VE, "mm", c["mm"])
            ve_op("tensor_scalar", r0, pb[5][0:1, :], INV_D, None, ALU.mult)
            ve_op("tensor_scalar", r1, pb[6][0:1, :], INV_D, None, ALU.mult, inc=True)
            pe_sum_free = ("dve", c["dve"])
            ve_op("tensor_tensor", r2, r0, r0, ALU.mult)
            ve_op("tensor_tensor", r3, r1, r2, ALU.subtract)
            ve_op("tensor_scalar", r2, r3, EPS_LN, None, ALU.add, inc=True)
            wait(SC, "dve", c["dve"])
            sc_op("activation", r3, r2, AF.Sqrt, inc=True)
            wait(VE, "act", c["act"])
            ve_op("reciprocal", r2, r3)
            ve_op("tensor_tensor", r3, r0, r2, ALU.mult, inc=True)
            prev_math_dve[0] = c["dve"]
            wait(PE, "dve", c["dve"])
            wait(PE, *pe_sum_free)
            mm(pb[5][:, :], negones_row, r3, start=True, stop=True, inc=True)
            mm(pb[6][:, :], ones_row, r2, start=True, stop=True, inc=True)
            wait(VE, "mm", c["mm"])
            ve_op("tensor_copy", nmr_b[:, tsl], pb[5][:, :])
            ve_op("tensor_copy", rs_b[:, tsl], pb[6][:, :], inc=True)
            bank_free[5] = ("dve", c["dve"])
            bank_free[6] = ("dve", c["dve"])

    def ffn_phase(compute_wsum):
        wait(VE, "act", c["act"])
        last_flush = [0]
        act3_last = [0, 0]
        PA = (ep_a, ep_e0)
        PY = (ep_y, ep_e1)
        for tc in range(4):
            csl = slice(tc * 512, tc * 512 + 512)
            flush_wait_act = 0
            for ot in range(16):
                p, w = slab_load(wff[ot])
                wait(PE, *w)
                if compute_wsum:
                    wsum_part1(p)
                bank = proj_group(
                    p, lambda k, tc=tc: xs[:, k * T + tc * 512: k * T + tc * 512 + 512])
                slab_last_use[p] = c["mm"]
                if compute_wsum:
                    wsum_part2(wsum_all[:, ot:ot + 1])
                a_r, y_r = PA[ot % 2], PY[ot % 2]
                wait(VE, "mm", c["mm"])
                if act3_last[ot % 2]:
                    wait(VE, "act", act3_last[ot % 2])
                ve_op("tensor_tensor", a_r, pb[bank][:, :], rs_b[:, csl], ALU.mult,
                      inc=True)
                bank_free[bank] = ("dve", c["dve"])
                ve_op("scalar_tensor_tensor", y_r, nmr_b[:, csl], wsum_all[:, ot:ot + 1],
                      a_r, ALU.mult, ALU.add, inc=True)
                wait(SC, "dve", c["dve"])
                if last_flush[0]:
                    wait(SC, "dve", last_flush[0])
                sc_op("activation", big[:, ot * 512:(ot + 1) * 512], y_r, AF.Identity,
                      bias=b_h[:, ot:ot + 1], scale=sc_h[:, ot:ot + 1], inc=True)
                act3_last[ot % 2] = c["act"]
                flush_wait_act = c["act"]
            wait(VE, "act", flush_wait_act)
            for ot in range(16):
                xsl = xs[:, ot * T + tc * 512: ot * T + tc * 512 + 512]
                ve_op("tensor_tensor", xsl, xsl, big[:, ot * 512:(ot + 1) * 512],
                      ALU.add, inc=(ot == 15))
            last_flush[0] = c["dve"]

    def fold_chunk(bank, tsl512, wsum_ap, dest, bias_ap, elu):
        wait(VE, "mm", c["mm"])
        ve_op("tensor_tensor", ep_a, pb[bank][:, 0:512], rs_b[:, tsl512], ALU.mult,
              inc=True)
        bank_free[bank] = ("dve", c["dve"])
        ve_op("scalar_tensor_tensor", ep_y, nmr_b[:, tsl512], wsum_ap, ep_a,
              ALU.mult, ALU.add)
        if elu:
            ve_op("tensor_scalar", ep_e0, ep_y, bias_ap, 0.0, ALU.add, ALU.min,
                  inc=True)
            wait(SC, "dve", c["dve"])
            sc_op("activation", ep_e1, ep_e0, AF.Exp, inc=True)
            ve_op("tensor_scalar", ep_a, ep_y, bias_ap, 0.0, ALU.add, ALU.max)
            wait(VE, "act", c["act"])
            ve_op("tensor_tensor", dest, ep_e1, ep_a, ALU.add, inc=True)
        else:
            ve_op("tensor_scalar", dest, ep_y, bias_ap, None, ALU.add, inc=True)

    def attn_phase():
        wait(VE, "act", c["act"])
        kfm = big[:, 0:512]
        kp_off = 512
        vt_off = 2560
        for tau in range(16):
            co = vt_off + tau * 129 + 128
            ve_op("memset", big[:, co:co + 1], 1.0, inc=(tau == 15))

        for h in range(16):
            for proj, wdram, bias_ap, wcol, tmo, bw in (
                ("k", wk, bk[:, h:h + 1], 0, kp_off, 128),
                ("v", wv, bv[:, h:h + 1], 1, vt_off, 129),
            ):
                p, w = slab_load(wdram[h])
                wait(PE, *w)
                wsum_part1(p)
                first = True
                for tca in range(4):
                    tsl = slice(tca * 512, tca * 512 + 512)
                    bank = proj_group(
                        p, lambda k, tca=tca: xs[:, k * T + tca * 512: k * T + tca * 512 + 512])
                    if tca == 3:
                        slab_last_use[p] = c["mm"]
                    if first:
                        wsum_part2(wsa[:, wcol:wcol + 1])
                        first = False
                    fold_chunk(bank, tsl, wsa[:, wcol:wcol + 1], kfm, bias_ap,
                               elu=(proj == "k"))
                    wait(PE, "dve", c["dve"])
                    for tpair in range(2):
                        for j in range(2):
                            tl = tpair * 2 + j
                            bnk = 5 + j
                            if bank_free[bnk] is not None:
                                wait(PE, *bank_free[bnk])
                            transpose(pb[bnk][:, 0:128], kfm[:, tl * 128:(tl + 1) * 128],
                                      eye, inc=(j == 1))
                        wait(SC, "mm", c["mm"])
                        for j in range(2):
                            tau = tca * 4 + tpair * 2 + j
                            co = tmo + tau * bw
                            sc_op("activation", big[:, co:co + 128],
                                  pb[5 + j][:, 0:128], AF.Identity, inc=(j == 1))
                        bank_free[5] = ("act", c["act"])
                        bank_free[6] = ("act", c["act"])
            if bank_free[7] is not None:
                wait(PE, *bank_free[7])
            wait(PE, "act", c["act"])
            for tau in range(16):
                mm(pb[7][:, 0:129],
                   big[:, kp_off + tau * 128: kp_off + (tau + 1) * 128],
                   big[:, vt_off + tau * 129: vt_off + tau * 129 + 129],
                   start=(tau == 0), stop=(tau == 15), inc=(tau == 15))
            wait(VE, "mm", c["mm"])
            ve_op("tensor_copy", kv_sb[:, h * 129:(h + 1) * 129], pb[7][:, 0:129],
                  inc=True)
            bank_free[7] = ("dve", c["dve"])

        wait(GP, "dve", c["dve"])
        c["g"] += 16
        GP.append(lambda e: e.dma_start(out=kvin[:, :], in_=kv_sb[:, :]).then_inc(SEM["g"], 16))
        wait(GP, "g", c["g"])
        c["cc"] += 1
        def cc_fn(e):
            e.collective_compute(
                "AllReduce", ALU.add,
                replica_groups=[[0, 1], [2, 3], [4, 5], [6, 7]],
                ins=[kvin.ap()], outs=[kvout.ap()],
            ).then_inc(SEM["cc"], 1)
        GP.append(cc_fn)
        wait(GP, "cc", c["cc"])
        c["g"] += 16
        GP.append(lambda e: e.dma_start(out=kv_sb[:, :], in_=kvout[:, :]).then_inc(SEM["g"], 16))
        kv_ready = c["g"]

        zr0a, zr0b = ep[32:33, 0:512], ep[64:65, 0:512]
        zr1a, zr1b = ep[0:1, 1536:2048], ep[0:1, 0:512]
        zrb_a, zrb_b = ep[:, 512:1024], ep[:, 1024:1536]
        for tc in range(4):
            for dt in range(16):
                p, w = slab_load(wq[dt])
                wait(PE, *w)
                if tc == 0:
                    wsum_part1(p)
                bank = proj_group(
                    p, lambda k, tc=tc: xs[:, k * T + tc * 512: k * T + tc * 512 + 512])
                slab_last_use[p] = c["mm"]
                if tc == 0:
                    wsum_part2(wsq_all[:, dt:dt + 1])
                fold_chunk(bank, slice(tc * 512, tc * 512 + 512),
                           wsq_all[:, dt:dt + 1], big[:, dt * 512:(dt + 1) * 512],
                           bq[:, dt:dt + 1], elu=True)
            qp_done = c["dve"]
            if tc == 0:
                wait(PE, "g", kv_ready)
            wait(PE, "dve", qp_done)
            nb = (4, 0)
            zb = (5, 7)
            bb = (6, 1)
            zr0 = (zr0a, zr0b)
            zr1 = (zr1a, zr1b)
            zrb = (zrb_a, zrb_b)
            zrec_done = [0] * 16
            zrecb_mm = [0] * 16

            def pe_numer_z(h):
                par = h % 2
                pe_use_bank(nb[par])
                mm(pb[nb[par]][:, 0:512], kv_sb[:, h * 129:h * 129 + 128],
                   big[:, h * 512:(h + 1) * 512], start=True, stop=True)
                pe_use_bank(zb[par])
                mm(pb[zb[par]][0:1, 0:512], kv_sb[:, h * 129 + 128:h * 129 + 129],
                   big[:, h * 512:(h + 1) * 512], start=True, stop=True, inc=True)

            def ve_zrec(h):
                par = h % 2
                wait(VE, "mm", c["mm"])
                ve_op("tensor_scalar", zr0[par], pb[zb[par]][0:1, 0:512], EPS_AT,
                      None, ALU.add)
                bank_free[zb[par]] = None
                ve_op("reciprocal", zr1[par], zr0[par], inc=True)
                zrec_done[h] = c["dve"]
                bank_free[zb[par]] = ("dve", c["dve"])

            def pe_zrecb(h):
                par = h % 2
                wait(PE, "dve", zrec_done[h])
                pe_use_bank(bb[par])
                mm(pb[bb[par]][:, 0:512], ones_row, zr1[par], start=True, stop=True,
                   inc=True)
                zrecb_mm[h] = c["mm"]

            def ve_attn(h):
                par = h % 2
                wait(VE, "mm", zrecb_mm[h])
                ve_op("tensor_copy", zrb[par], pb[bb[par]][:, 0:512])
                bank_free[bb[par]] = ("dve", c["dve"] + 1)
                ve_op("tensor_tensor", big[:, h * 512:(h + 1) * 512],
                      pb[nb[par]][:, 0:512], zrb[par], ALU.mult, inc=True)
                bank_free[nb[par]] = ("dve", c["dve"])

            pe_numer_z(0)
            ve_zrec(0)
            for h in range(1, 16):
                pe_numer_z(h)
                pe_zrecb(h - 1)
                ve_zrec(h)
                ve_attn(h - 1)
            pe_zrecb(15)
            ve_attn(15)
            attn_done = c["dve"]
            wait(PE, "dve", attn_done)
            for ct in range(16):
                p, w = slab_load(wo[ct])
                wait(PE, *w)
                bank = proj_group(
                    p, lambda k: big[:, k * 512:(k + 1) * 512])
                slab_last_use[p] = c["mm"]
                xsl = xs[:, ct * T + tc * 512: ct * T + tc * 512 + 512]
                wait(VE, "mm", c["mm"])
                ve_op("scalar_tensor_tensor", xsl, pb[bank][:, 0:512], bo[:, ct:ct + 1],
                      xsl, ALU.add, ALU.add, inc=True)
                bank_free[bank] = ("dve", c["dve"])

    if "statsdump" in DO_PHASES:
        stats_phase()
        wait(SY, "dve", c["dve"])
        c["st"] += 32
        SY.append(lambda e: e.dma_start(out=x_out[:, 0:2048], in_=nmr_b[:, :]).then_inc(SEM["st"], 16))
        SY.append(lambda e: e.dma_start(out=x_out[:, 2048:4096], in_=rs_b[:, :]).then_inc(SEM["st"], 16))
    if "ffn1" in DO_PHASES:
        stats_phase()
        ffn_phase(compute_wsum=True)
    if "attn" in DO_PHASES:
        stats_phase()
        attn_phase()
    if "ffn2" in DO_PHASES:
        stats_phase()
        ffn_phase(compute_wsum=("ffn1" not in DO_PHASES))

    if "statsdump" not in DO_PHASES:
        wait(SY, "dve", c["dve"])
        c["st"] += 16
        SY.append(lambda e: e.dma_start(out=x_out[:, :], in_=xs[:, :]).then_inc(SEM["st"], 16))
    wait(SY, "st", c["st"])

    with stk, nc.Block() as block:
        @block.sync
        def _(sync):
            for f in SY:
                f(sync)

        @block.tensor
        def _(tensor):
            for f in PE:
                f(tensor)

        @block.vector
        def _(vector):
            for f in VE:
                f(vector)

        @block.scalar
        def _(scalar):
            for f in SC:
                f(scalar)

        @block.gpsimd
        def _(gpsimd):
            for f in GP:
                f(gpsimd)

    return nc


def _tile_w(w):
    wT = np.ascontiguousarray(np.asarray(w, dtype=np.float32).T)
    return np.ascontiguousarray(
        wT.reshape(16, 128, 16, 128).transpose(2, 1, 0, 3).reshape(16, 128, 2048))


def _tile_b(b):
    return np.ascontiguousarray(np.asarray(b, dtype=np.float32).reshape(16, 128).T)


def kernel(L0_attn_hidden_states, L0_attn_q_w, L0_attn_q_b, L0_attn_k_w,
           L0_attn_k_b, L0_attn_v_w, L0_attn_v_b, L0_attn_o_w, L0_attn_o_b,
           L0_ffn_weight, L0_ffn_scale_weight, L0_ffn_bias):
    x = np.asarray(L0_attn_hidden_states, dtype=np.float32)
    bias6 = np.stack([
        _tile_b(L0_ffn_bias), _tile_b(L0_ffn_scale_weight), _tile_b(L0_attn_q_b),
        _tile_b(L0_attn_k_b), _tile_b(L0_attn_v_b), _tile_b(L0_attn_o_b)])
    common = {
        "wff": _tile_w(L0_ffn_weight),
        "wq": _tile_w(L0_attn_q_w),
        "wk": _tile_w(L0_attn_k_w),
        "wv": _tile_w(L0_attn_v_w),
        "wo": _tile_w(L0_attn_o_w),
        "bias6": np.ascontiguousarray(bias6),
        "eye": np.eye(128, dtype=np.float32),
    }
    in_maps = []
    for i in range(8):
        b, half = i // 2, i % 2
        xc = x[b, half * T:(half + 1) * T, :]
        xf = np.ascontiguousarray(
            xc.T.reshape(16, 128, T).transpose(1, 0, 2).reshape(128, KC * T))
        in_maps.append(dict(common, x_in=xf))
    nc = build_nc()
    res = run_bass_kernel_spmd(nc, in_maps, core_ids=list(range(8)))
    out = np.empty((4, 4096, D), dtype=np.float32)
    for i in range(8):
        b, half = i // 2, i % 2
        o = res.results[i]["x_out"]
        xT = o.reshape(128, 16, T).transpose(1, 0, 2).reshape(D, T)
        out[b, half * T:(half + 1) * T, :] = xT.T
    return out
